# version 35
# baseline (speedup 1.0000x reference)
import numpy as np
import ml_dtypes
import contextlib
import concourse.bass as bass
import concourse.mybir as mybir
from concourse.alu_op_type import AluOpType
from concourse.bass_utils import run_bass_kernel_spmd

F32 = mybir.dt.float32
BF16 = mybir.dt.bfloat16
FP8 = mybir.dt.float8e4
AF = mybir.ActivationFunctionType
AX = mybir.AxisListType.X
AXC = mybir.AxisListType.C
MMPM = mybir.MatmulPerfMode

P = 128
S = 4096
E = 1024
D = 64
NT = S // P
NPAIR = 4
NG = 8
EPS = 1e-5
FP8MAX = 448.0

ENGINES = ("sync", "gpsimd", "vector", "scalar", "tensor")


class Plan:
    def __init__(self):
        self.ops = {e: [] for e in ENGINES}
        self.cnt = {}

    def inc(self, name, n=1):
        self.cnt.setdefault(name, 0)
        self.cnt[name] += n
        return self.cnt[name]

    def val(self, name):
        return self.cnt.setdefault(name, 0)

    def op(self, eng, emit, waits=(), incs=()):
        for nm, n in incs:
            self.inc(nm, n)
        self.ops[eng].append((emit, tuple(waits), tuple(incs)))


def build(n_cores=8):
    nc = bass.Bass("TRN2", target_bir_lowering=False, debug=False)

    xb = nc.dram_tensor("xb", [S, E], F32, kind="ExternalInput").ap()
    xresd = nc.dram_tensor("xresd", [2048, E], F32, kind="ExternalInput").ap()
    wqkvd = nc.dram_tensor("wqkvd", [12 * P, D], F32, kind="ExternalInput").ap()
    wod = nc.dram_tensor("wod", [E, E], F32, kind="ExternalInput").ap()
    wftd = nc.dram_tensor("wftd", [E, E], F32, kind="ExternalInput").ap()
    biasd = nc.dram_tensor("biasd", [P, E], F32, kind="ExternalInput").ap()
    trid = nc.dram_tensor("trid", [P, P], BF16, kind="ExternalInput").ap()
    idend = nc.dram_tensor("idend", [P, P], F32, kind="ExternalInput").ap()
    outd = nc.dram_tensor("outd", [2048, E], F32, kind="ExternalOutput").ap()

    x2spd = nc.dram_tensor("x2spd", [2048, E], F32)
    amxd = nc.dram_tensor("amxd", [P], F32)
    wamxd = nc.dram_tensor("wamxd", [P], F32)
    scld = nc.dram_tensor("scld", [8], F32)
    cc_in = nc.dram_tensor("cc_in", [1, 8], F32)
    cc_out = nc.dram_tensor("cc_out", [1, 8], F32)
    cw_in = nc.dram_tensor("cw_in", [1, 8], F32)
    cw_out = nc.dram_tensor("cw_out", [1, 8], F32)

    pl = Plan()
    es = contextlib.ExitStack()
    with es:
        Een = es.enter_context

        def sbt(name, shape, dt):
            return Een(nc.sbuf_tensor(name, shape, dt))

        def pst(name, shape, dt):
            return Een(nc.psum_tensor(name, shape, dt))

        xt = [sbt("xt%d" % i, [P, E], F32) for i in range(4)]
        bn12s = [sbt("bn12%d" % i, [P, 12], F32) for i in range(4)]
        mvs = [sbt("mv%d" % i, [P, 2], F32) for i in range(4)]
        ceps = sbt("ceps", [P, 1], F32)
        rstd1s = [sbt("rstd1%d" % i, [P, 1], F32) for i in range(4)]
        xnf = [sbt("xnf%d" % i, [P, 512], F32) for i in range(2)]
        xnT = [sbt("xnT%d" % p, [P, S], BF16) for p in range(NPAIR)]
        wqkvb = sbt("wqkvb", [P, 768], BF16)
        dd = sbt("dd", [P, E], F32)
        wqkvf = dd[0:P, 0:768]
        xqf = dd
        wob = sbt("wob", [P, 8 * E], BF16)
        wq8 = sbt("wq8", [P, 8 * E], FP8)
        biasr = sbt("biasr", [P, E], F32)
        tri = sbt("tri", [P, P], BF16)
        iden = sbt("iden", [P, P], F32)
        QT2 = [sbt("QT0", [P, S], BF16), sbt("QT1", [P, S], BF16)]
        KT2 = [sbt("KT0", [P, S], BF16), sbt("KT1", [P, S], BF16)]
        VS2 = [[sbt("VA0", [P, 32 * 65], BF16), sbt("VB0", [P, 32 * 65], BF16)],
               [sbt("VA1", [P, 32 * 65], BF16), sbt("VB1", [P, 32 * 65], BF16)]]
        PT = [sbt("PT%d" % i, [P, 1024], BF16) for i in range(4)]
        cat = sbt("cat", [P, NT * 512], BF16)
        rinv4 = sbt("rinv4", [P, 16], F32)
        stats_m = sbt("stats_m", [P, 16], F32)
        stats_r = sbt("stats_r", [P, 16], F32)
        amaxp = sbt("amaxp", [P, 16], F32)
        wamaxp = sbt("wamaxp", [P, 8], F32)
        xres = [sbt("xres0", [P, E], F32), sbt("xres1", [P, E], F32)]
        x2t = [sbt("x2t0", [P, E], F32), sbt("x2t1", [P, E], F32)]
        scal = sbt("scal", [1, 16], F32)
        sc128 = sbt("sc128", [P, 4], F32)
        cc8 = sbt("cc8", [1, 8], F32)
        amx1 = sbt("amx1", [1, P], F32)
        q8t = [sbt("q8t0", [P, E], FP8), sbt("q8t1", [P, E], FP8),
               sbt("q8t2", [P, E], FP8)]
        xn32 = [sbt("xn32_%d" % i, [P, E], F32) for i in range(2)]
        ones1r = sbt("ones1r", [1, P], F32)
        fout = x2t

        nc.const_aps.register(F32, EPS, ceps[:])

        sc2 = [pst("sc2a", [P, 1024], F32), pst("sc2b", [P, 1024], F32)]
        av1h = [pst("av1a", [P, 512], F32), pst("av1b", [P, 512], F32)]
        pw = [pst("pw0", [P, 512], F32), pst("pw1", [P, 512], F32)]

        sems = {}
        for n in ("sx0", "sx1", "sx2", "sx3", "scst", "sxr0", "sxr1", "ssp0", "ssp1",
                  "sgc", "sout0", "sout1", "scc", "s_stats", "s_rstd",
                  "s_ln", "s_tr", "s_xnT", "s_qkv", "s_qev", "s_sc",
                  "s_exp", "s_msk", "s_av", "s_avev", "s_wo", "s_woev",
                  "s_ff", "s_ffev", "s_misc", "s_gp", "s_wc", "s_wob"):
            sems[n] = Een(nc.semaphore(n))
            pl.val(n)

        pwev = [("s_misc", 0), ("s_misc", 0)]
        xt_free = [[], [], [], []]
        xnT_tok = [0] * NT
        apply_tok = [0] * NT
        stats_tok = [0] * NT
        rstd_par_tok = [0, 0, 0, 0]
        appl_par_tok = [0, 0, 0, 0]
        tr_tok2 = [0] * NT

        def dma(eng_name, dst, src, waits=(), sem="scst"):
            pl.op(eng_name, lambda e, d=dst, s=src: e.dma_start(out=d, in_=s),
                  waits, [(sem, 16)])
            return pl.val(sem)

        def drain(eng_name):
            pl.op(eng_name, lambda e: e.drain())

        dma("gpsimd", tri[:], trid, (), "scst")
        dma("gpsimd", iden[:], idend, (), "scst")
        dma("gpsimd", biasr[:], biasd, (), "scst")
        dma("scalar", wqkvf,
            bass.AP(wqkvd.tensor, 0, [[64, P], [8192, 12], [1, 64]]), (), "scst")
        t_tri = t_iden = t_bias = t_wqkv = 64
        pl.op("gpsimd", lambda e: e.memset(ceps[:], EPS), (), [("s_gp", 1)])
        t_ceps = pl.val("s_gp")
        pl.op("gpsimd", lambda e: e.memset(ones1r[:], 1.0), (), [("s_gp", 1)])
        t_ones = pl.val("s_gp")
        t_cw = dma("gpsimd", cw_in.ap()[0:1, 0:1], ceps[0:1, 0:1],
                   [("s_gp", t_ceps)], "sgc")
        pl.op("gpsimd", lambda e: e.collective_compute(
            "AllReduce", AluOpType.max,
            replica_groups=[list(range(n_cores))],
            ins=[cw_in.ap()[0:1, 0:1].opt()],
            outs=[cw_out.ap()[0:1, 0:1].opt()]),
            [("sgc", t_cw)], [("scc", 1)])
        for vpair in VS2:
            for v in vpair:
                pl.op("gpsimd", lambda e, vv=v: e.memset(
                    bass.AP(vv, 64, [[65 * 32, P], [65, 32], [1, 1]]), 1.0))
        pl.op("gpsimd", lambda e: e.tensor_copy(wqkvb[:], wqkvf),
              [("scst", t_wqkv)], [("s_gp", 1)])
        t_wqkvb = pl.val("s_gp")

        state = {"qev": 0, "wo_cnt": 0, "rstd_tok": pl.val("s_rstd")}
        last_sc_by_pair = {}
        last_av_by_pair = {}
        qt_tok = {}
        kt_tok = {}
        v_tok = {}
        avev_partial = {}
        sp_tok = [0] * 16
        woev_tok = [0] * 16
        workq = []

        def pump(n=1, avail=10 ** 9, allow_heavy=True):
            cnt = 0
            while (workq and cnt < n and workq[0][0] <= avail
                   and (allow_heavy or not workq[0][1])):
                workq.pop(0)[2]()
                cnt += 1

        def flush():
            while workq:
                workq.pop(0)[2]()

        def enqueue_qkv(p):
            bp = p % 2
            QT, KT, VSp = QT2[bp], KT2[bp], VS2[bp]
            sc_war = last_sc_by_pair.get(p - 2, 0)
            av_war = last_av_by_pair.get(p - 2, 0)

            def qt_slice(proj, dsttile, sl):
                def thunk():
                    co = (proj * 4 + p) * D
                    tokdict = qt_tok if proj == 0 else kt_tok
                    tk = xnT_tok[min(sl * 4 + 3, NT - 1)]
                    lw = [pwev[0], pwev[1], ("s_qev", tk[0]),
                          ("s_xnT", tk[1]), ("s_gp", t_wqkvb)]
                    pl.op("tensor", lambda e, c=co, s0=sl: e.matmul(
                        pw[0][0:64, 0:512], wqkvb[0:64, c:c + D],
                        xnT[p][0:64, s0 * 512:(s0 + 1) * 512],
                        start=True, stop=True), lw)
                    pl.op("tensor", lambda e, c=co, s0=sl: e.matmul(
                        pw[1][0:64, 0:512], wqkvb[64:128, c:c + D],
                        xnT[p][64:128, s0 * 512:(s0 + 1) * 512],
                        start=True, stop=True), (), [("s_qkv", 1)])
                    tq = pl.val("s_qkv")
                    wv = [("s_qkv", tq)]
                    if sc_war:
                        wv.append(("s_sc", sc_war))
                    pl.op("vector", lambda e, d=dsttile, s0=sl: e.tensor_copy(
                        d[0:64, s0 * 512:(s0 + 1) * 512], pw[0][0:64, 0:512]),
                        wv, [("s_qev", 1)])
                    pwev[0] = ("s_qev", pl.val("s_qev"))
                    if p == 0:
                        pl.op("scalar", lambda e, d=dsttile, s0=sl: e.copy(
                            d[64:128, s0 * 512:(s0 + 1) * 512],
                            pw[1][0:64, 0:512]),
                            wv, [("s_xnT", 1)])
                        pwev[1] = ("s_xnT", pl.val("s_xnT"))
                        tokdict[(p, sl)] = (pl.val("s_qev"), pl.val("s_xnT"))
                    else:
                        pl.op("vector", lambda e, d=dsttile, s0=sl: e.tensor_copy(
                            d[64:128, s0 * 512:(s0 + 1) * 512],
                            pw[1][0:64, 0:512]),
                            (), [("s_qev", 1)])
                        pwev[1] = ("s_qev", pl.val("s_qev"))
                        tokdict[(p, sl)] = (pl.val("s_qev"), 0)
                return thunk

            def v_round(rnd):
                def thunk():
                    co = (2 * 4 + p) * D
                    for j in range(4):
                        k = rnd * 4 + j
                        tk = xnT_tok[min(k, NT - 1)]
                        lw = [("s_qev", tk[0]), ("s_xnT", tk[1])]
                        if j == 0:
                            lw += [pwev[0], pwev[1]]
                        pl.op("tensor", lambda e, kk=k, jj=j, c=co: e.matmul(
                            pw[0][0:P, jj * 64:(jj + 1) * 64],
                            xnT[p][0:64, kk * P:(kk + 1) * P],
                            wqkvb[0:64, c:c + D], start=True, stop=True), lw)
                        pl.op("tensor", lambda e, kk=k, jj=j, c=co: e.matmul(
                            pw[1][0:P, jj * 64:(jj + 1) * 64],
                            xnT[p][64:128, kk * P:(kk + 1) * P],
                            wqkvb[64:128, c:c + D], start=True, stop=True),
                            (), [("s_qkv", 1)] if j == 3 else ())
                    tv = pl.val("s_qkv")
                    for hip in range(2):
                        wv = [("s_qkv", tv)]
                        if av_war and rnd == 0:
                            wv.append(("s_av", av_war))
                        pl.op("vector", lambda e, h=hip, r0=rnd: e.tensor_copy(
                            bass.AP(VSp[h], r0 * 4 * 65,
                                    [[65 * 32, P], [65, 4], [1, 64]]),
                            bass.AP(pw[h], 0, [[512, P], [64, 4], [1, 64]])),
                            wv, [("s_qev", 1)])
                        pwev[hip] = ("s_qev", pl.val("s_qev"))
                    v_tok[(p, rnd)] = pl.val("s_qev")
                return thunk

            for sl in range(8):
                workq.append((4 * sl + 3, False, qt_slice(0, QT, sl)))
                workq.append((4 * sl + 3, False, qt_slice(1, KT, sl)))
                workq.append((4 * sl + 3, False, v_round(sl)))


        enqueue_qkv(0)
        ln1_done = {"n": 0}

        def ln1_statpart(i):
            par = i % 4
            buf = xt[par]
            t_x = dma("sync", buf[:], xb[i * P:(i + 1) * P, :], xt_free[par],
                      "sx%d" % par)
            xt_free[par] = []
            bn12, mv, rstd1 = (bn12s[par], mvs[par], rstd1s[par])
            w = [("sx%d" % par, t_x)]
            if i >= 4:
                w += [("s_ln", appl_par_tok[par]),
                      ("s_rstd", rstd_par_tok[par])]
            pl.op("vector", lambda e, b_=buf, z=bn12: e.bn_stats(
                z[:, 0:6], b_[:, 0:512]), w)
            pl.op("vector", lambda e, b_=buf, z=bn12: e.bn_stats(
                z[:, 6:12], b_[:, 512:1024]), (), [("s_misc", 1)])
            t_bs = pl.val("s_misc")
            pl.op("vector", lambda e, z=bn12, m=mv: e.bn_aggr(m[:], z[:]),
                  [("s_misc", t_bs)], [("s_stats", 1)])
            stats_tok[i] = pl.val("s_stats")
            xt_free[par].append(("s_stats", stats_tok[i]))
            pl.op("scalar", lambda e, m=mv, r_=rstd1: e.activation(
                r_[:], m[:, 1:2], AF.Ln, bias=EPS),
                [("s_stats", stats_tok[i]), ("s_gp", t_ceps)],
                [("s_rstd", 1)])
            t_ln = pl.val("s_rstd")
            pl.op("scalar", lambda e, r_=rstd1: e.activation(
                r_[:], r_[:], AF.Exp, scale=-0.5), [("s_rstd", t_ln)],
                [("s_rstd", 1)])
            rstd_par_tok[par] = pl.val("s_rstd")

        def ln1_applypart(i):
            par = i % 4
            buf = xt[par]
            mv, rstd1 = mvs[par], rstd1s[par]
            ob = xnf[i % 2]
            w2 = [("s_rstd", rstd_par_tok[par])]
            if i >= 2:
                w2.append(("s_tr", tr_tok2[i - 2]))
            pl.op("vector", lambda e, b_=buf, o=ob, m=mv, r_=rstd1:
                  e.tensor_scalar(o[:], b_[:, 0:512], m[:, 0:1],
                                  r_[:], AluOpType.subtract,
                                  AluOpType.mult),
                  w2, [("s_ln", 1)])
            apply_tok[i] = pl.val("s_ln")
            appl_par_tok[par] = apply_tok[i]
            xt_free[par].append(("s_ln", apply_tok[i]))
            for cb in range(4):
                wv = [("s_ln", apply_tok[i]), ("scst", t_iden),
                      pwev[cb % 2]]
                pl.op("tensor", lambda e, o=ob, c=cb: e.transpose(
                    pw[c % 2][0:P, (c // 2) * P:(c // 2) * P + P],
                    o[:, c * P:(c + 1) * P], iden[:]),
                    wv, [("s_tr", 1)])
            tr_tok2[i] = pl.val("s_tr")
            for cb in (0, 2):
                pl.op("scalar", lambda e, c=cb, ii=i: e.copy(
                    xnT[c][0:P, ii * P:(ii + 1) * P],
                    pw[0][0:P, (c // 2) * P:(c // 2) * P + P]),
                    [("s_tr", tr_tok2[i])], [("s_xnT", 1)])
            t_ev_d = pl.val("s_xnT")
            for cb in (1, 3):
                pl.op("scalar", lambda e, c=cb, ii=i: e.copy(
                    xnT[c][0:P, ii * P:(ii + 1) * P],
                    pw[1][0:P, (c // 2) * P:(c // 2) * P + P]),
                    [("s_tr", tr_tok2[i])], [("s_xnT", 1)])
            t_ev_a = pl.val("s_xnT")
            xnT_tok[i] = (0, t_ev_a)
            pwev[0] = ("s_xnT", t_ev_d)
            pwev[1] = ("s_xnT", t_ev_a)
            pump(2, avail=i - 1)
            ln1_done["n"] = i + 1

        ln1_started = {"n": 0}

        def ln1_advance():
            if ln1_started["n"] < NT:
                ln1_statpart(ln1_started["n"])
                ln1_started["n"] += 1
                if ln1_started["n"] < NT:
                    ln1_statpart(ln1_started["n"])
                    ln1_started["n"] += 1
            ln1_applypart(ln1_done["n"])

        def ensure_tiles(n):
            while ln1_done["n"] < min(n, NT):
                nxt = ln1_done["n"]
                while ln1_started["n"] < min(nxt + 2, NT):
                    ln1_statpart(ln1_started["n"])
                    ln1_started["n"] += 1
                ln1_applypart(nxt)

        def ln1_trickle():
            for _ in range(2):
                if ln1_started["n"] < NT:
                    ln1_statpart(ln1_started["n"])
                    ln1_started["n"] += 1
                if (ln1_done["n"] < ln1_started["n"] - 1
                        and ln1_done["n"] < NT):
                    ln1_applypart(ln1_done["n"])

        wstate = {"wob_tok": 0, "t_wqb": 0, "t_sw2": 0, "wamax_done": 0}

        def enqueue_weight_prep():
            ln1_free = [("s_ln", apply_tok[NT - 1]),
                        ("s_stats", stats_tok[NT - 1])]
            wtok = {}

            def wo_dma(j):
                def thunk():
                    par = j % 4
                    wtok[("o", j)] = dma(
                        "sync", xt[par][:], wod[j * P:(j + 1) * P, :],
                        xt_free[par] + (ln1_free if j < 4 else []),
                        "sx%d" % par)
                    xt_free[par] = []
                return thunk

            def wo_copy(j):
                def thunk():
                    par = j % 4
                    pl.op("vector", lambda e, jj=j, b_=xt[par]: e.tensor_copy(
                        wob[:, jj * E:(jj + 1) * E], b_[:]),
                        [("sx%d" % par, wtok[("o", j)])], [("s_wob", 1)])
                    xt_free[par].append(("s_wob", pl.val("s_wob")))
                    if j == 7:
                        wstate["wob_tok"] = pl.val("s_wob")
                return thunk

            def wf_dma(j):
                def thunk():
                    par = j % 4
                    wtok[("f", j)] = dma(
                        "sync", xt[par][:], wftd[j * P:(j + 1) * P, :],
                        xt_free[par], "sx%d" % par)
                    xt_free[par] = []
                return thunk

            def wf_use(j):
                def thunk():
                    par = j % 4
                    pl.op("vector", lambda e, jj=j, b_=xt[par]: e.tensor_reduce(
                        wamaxp[:, jj:jj + 1], b_[:], AX, AluOpType.max,
                        apply_absolute_value=True),
                        [("sx%d" % par, wtok[("f", j)])], [("s_stats", 1)])
                    xt_free[par].append(("s_stats", pl.val("s_stats")))
                return thunk

            def wfin_thunk():
                t_wprev = pl.val("s_stats")
                pl.op("vector", lambda e: e.tensor_reduce(
                    wamaxp[:, 0:1], wamaxp[:], AX, AluOpType.max),
                    [("s_stats", t_wprev)], [("s_stats", 1)])
                t_wam2 = pl.val("s_stats")
                t_wb1 = dma("gpsimd", wamxd.ap(), wamaxp[:, 0:1],
                            [("s_stats", t_wam2)], "sgc")
                t_wb2 = dma("gpsimd", amx1[0:1, :],
                            bass.AP(wamxd, 0, [[0, 1], [1, P]]),
                            [("sgc", t_wb1)], "sgc")
                pl.op("vector", lambda e: e.tensor_reduce(
                    scal[0:1, 1:2], amx1[0:1, :], AX, AluOpType.max),
                    [("sgc", t_wb2)], [("s_misc", 1)])
                t_wmx = pl.val("s_misc")
                pl.op("vector", lambda e: e.reciprocal(scal[0:1, 5:6],
                                                       scal[0:1, 1:2]),
                      [("s_misc", t_wmx)], [("s_misc", 1)])
                t_wrc = pl.val("s_misc")
                pl.op("vector", lambda e: e.tensor_scalar(
                    scal[0:1, 4:5], scal[0:1, 5:6], FP8MAX / 2, None,
                    AluOpType.mult), [("s_misc", t_wrc)], [("s_stats", 1)])
                t_swh = pl.val("s_stats")
                t_sw1 = dma("gpsimd", bass.AP(scld, 4, [[0, 1], [1, 1]]),
                            scal[0:1, 4:5], [("s_stats", t_swh)], "sgc")
                wstate["t_sw2"] = dma("gpsimd", sc128[:, 2:3],
                                      bass.AP(scld, 4, [[0, P], [1, 1]]),
                                      [("sgc", t_sw1)], "sgc")

            def wq_dma(j):
                def thunk():
                    par = j % 4
                    wtok[("q", j)] = dma(
                        "sync", xt[par][:], wftd[j * P:(j + 1) * P, :],
                        xt_free[par], "sx%d" % par)
                    xt_free[par] = []
                return thunk

            def wq_use(j):
                def thunk():
                    par = j % 4
                    wq_ = [("sx%d" % par, wtok[("q", j)]),
                           ("sgc", wstate["t_sw2"])]
                    pl.op("vector", lambda e, jj=j, b_=xt[par]:
                          e.tensor_scalar(wq8[:, jj * E:(jj + 1) * E], b_[:],
                                          sc128[:, 2:3], None,
                                          AluOpType.mult),
                          wq_, [("s_stats", 1)])
                    t_q = pl.val("s_stats")
                    xt_free[par].append(("s_stats", t_q))
                    if j == 7:
                        wstate["t_wqb"] = t_q
                return thunk

            def stagger(dmas, uses):
                for j in range(len(dmas)):
                    workq.append((-1, False, dmas[j]))
                    if j > 0:
                        workq.append((-1, False, uses[j - 1]))
                workq.append((-1, False, uses[-1]))

            stagger([wo_dma(j) for j in range(8)],
                    [wo_copy(j) for j in range(8)])
            stagger([wf_dma(j) for j in range(8)],
                    [wf_use(j) for j in range(8)])
            workq.append((-1, False, wfin_thunk))
            stagger([wq_dma(j) for j in range(8)],
                    [wq_use(j) for j in range(8)])

        slot_idx = 0
        exp_tok = {}
        exph0_tok = {}
        av_tok = {}
        avh0_tok = {}
        prev_div = [0, 0]
        sc_tok = {}
        avev_tok = {}
        def enqueue_wo(p, r):
            t = p * 4 + r
            wcnt = state["wo_cnt"]
            state["wo_cnt"] = wcnt + 1
            catev = avev_tok[(p, 2 * r + 1)]
            cell = {}

            def mm_thunk(jj):
                def thunk():
                    if jj == 0:
                        cell["t_xr"] = dma(
                            "sync", xres[wcnt % 2][:],
                            xresd[t * P:(t + 1) * P, :],
                            [("s_woev", woev_tok[wcnt - 2])] if wcnt >= 2
                            else [], "sxr%d" % (wcnt % 2))
                    for nb in range(2):
                        w2 = [("s_avev", catev), ("s_wob", wstate["wob_tok"])]
                        if jj == 0:
                            w2.append(pwev[nb])
                        pl.op("tensor", lambda e, n=nb, j=jj: e.matmul(
                            pw[n][:],
                            cat[0:P, (8 * r + j) * 512 + p * P:
                                (8 * r + j) * 512 + p * P + P],
                            wob[0:P, j * E + n * 512:j * E + n * 512 + 512],
                            start=(j == 0), stop=(j == 7)),
                            w2, [("s_wo", 1)] if (jj == 7 and nb == 1) else ())
                    if jj == 7:
                        cell["t_wo"] = pl.val("s_wo")
                return thunk

            def evac_thunk():
                bnb = wcnt % 2
                bb = x2t[bnb]
                bn12, mv, rstd1 = (bn12s[bnb], mvs[bnb], rstd1s[bnb])
                w3 = [("s_wo", cell["t_wo"]),
                      ("sxr%d" % (wcnt % 2), cell["t_xr"]),
                      ("s_rstd", state["rstd_tok"]),
                      ("s_gp", t_wqkvb),
                      ("s_stats", state.get("amax_tok", 0))]
                if wcnt >= 2:
                    w3.append(("ssp%d" % (wcnt % 2), sp_tok[wcnt - 2]))
                pl.op("vector", lambda e, b_=bb, xr=xres[wcnt % 2]:
                      e.tensor_tensor(b_[:, 0:512], pw[0][:], xr[:, 0:512],
                                      AluOpType.add), w3, [("s_misc", 1)])
                pwev[0] = ("s_misc", pl.val("s_misc"))
                pl.op("vector", lambda e, b_=bb, xr=xres[wcnt % 2]:
                      e.tensor_tensor(b_[:, 512:1024], pw[1][:],
                                      xr[:, 512:1024], AluOpType.add),
                      (), [("s_misc", 1)])
                pwev[1] = ("s_misc", pl.val("s_misc"))
                t_x2w = pl.val("s_misc")
                pl.op("vector", lambda e, b_=bb, z=bn12: e.bn_stats(
                    z[:, 0:6], b_[:, 0:512]), [("s_misc", t_x2w)])
                pl.op("vector", lambda e, b_=bb, z=bn12: e.bn_stats(
                    z[:, 6:12], b_[:, 512:1024]), (), [("s_misc", 1)])
                t_bs2 = pl.val("s_misc")
                pl.op("vector", lambda e, z=bn12, m=mv: e.bn_aggr(m[:], z[:]),
                      [("s_misc", t_bs2)], [("s_stats", 1)])
                t_st = pl.val("s_stats")
                pl.op("vector", lambda e, tt=t, m=mv: e.tensor_copy(
                    stats_m[:, tt:tt + 1], m[:, 0:1]), [("s_stats", t_st)])
                pl.op("scalar", lambda e, m=mv, r_=rstd1: e.activation(
                    r_[:], m[:, 1:2], AF.Ln, bias=EPS),
                    [("s_stats", t_st), ("s_gp", t_ceps)], [("s_rstd", 1)])
                t_ln2 = pl.val("s_rstd")
                pl.op("scalar", lambda e, tt=t, r_=rstd1: e.activation(
                    stats_r[:, tt:tt + 1], r_[:], AF.Exp, scale=-0.5),
                    [("s_rstd", t_ln2)], [("s_rstd", 1)])
                state["rstd_tok"] = pl.val("s_rstd")
                pl.op("vector", lambda e, b_=bb, m=mv: e.tensor_scalar(
                    dd[:], b_[:], m[:, 0:1], None, AluOpType.subtract),
                    (), [("s_misc", 1)])
                t_dd = pl.val("s_misc")
                pl.op("vector", lambda e, tt=t: e.tensor_reduce(
                    amaxp[:, tt:tt + 1], dd[:], AX, AluOpType.max,
                    apply_absolute_value=True), [("s_misc", t_dd)],
                    [("s_misc", 1)])
                t_amx = pl.val("s_misc")
                pl.op("vector", lambda e, tt=t: e.tensor_scalar(
                    amaxp[:, tt:tt + 1], amaxp[:, tt:tt + 1],
                    stats_r[:, tt:tt + 1], None, AluOpType.mult),
                    [("s_rstd", state["rstd_tok"]), ("s_misc", t_amx)],
                    [("s_woev", 1)])
                woev_tok[wcnt] = pl.val("s_woev")
                state["amax_tok"] = pl.val("s_stats")
                sp_tok[wcnt] = dma("gpsimd",
                                   x2spd.ap()[t * P:(t + 1) * P, :], bb[:],
                                   [("s_woev", woev_tok[wcnt])],
                                   "ssp%d" % (wcnt % 2))

            for jj in range(8):
                workq.append((-1, False, mm_thunk(jj)))
            workq.append((-1, True, evac_thunk))

        state["rstd_tok"] = pl.val("s_rstd")

        enqueue_qkv(1)
        sched = ([(0, g) for g in range(7)] + [(1, 0), (0, 7)]
                 + [(1, g) for g in range(1, 8)]
                 + [(2, g) for g in range(8)] + [(3, g) for g in range(8)])
        seen_pair2 = False
        for p, g in sched:
            if p == 2 and not seen_pair2:
                seen_pair2 = True
                enqueue_qkv(3)
            if p == 0:
                ensure_tiles(4 * g + 4)
            if ln1_done["n"] < NT:
                pump(100, avail=ln1_done["n"] - 1)
            pend = []

            def emit_av(sl0, ctx):
                kk0, gg0, qsm0, pt0, diag0, msk0 = ctx
                for hip in range(2):
                    for qs in range(qsm0, 4):
                        wav = []
                        if qs == qsm0:
                            wav.append(("s_exp", exp_tok[sl0]))
                            wav.append(("s_qev", v_tok[(p, kk0 // 4)]))
                            if diag0:
                                wav.append(("s_msk", msk0))
                            if kk0 == 0:
                                wav.append(("s_avev", prev_div[hip]))
                        last_of_hip = (qs == 3)
                        pl.op("tensor", lambda e, h=hip, q=qs, kk=kk0,
                              gg=gg0, pt=pt0, bp=p % 2: e.matmul(
                                  av1h[h][0:P, q * 65:q * 65 + 65],
                                  pt[0:P, h * 512 + q * P:
                                     h * 512 + (q + 1) * P],
                                  VS2[bp][h][0:P, kk * 65:(kk + 1) * 65],
                                  start=(kk == 0 and q == 0),
                                  stop=(kk == 4 * gg + 3 and q == 3)),
                              wav, [("s_av", 1)] if last_of_hip else ())
                        if last_of_hip and hip == 0:
                            avh0_tok[sl0] = pl.val("s_av")
                av_tok[sl0] = pl.val("s_av")

            for k in range(4 * g + 4):
                sl = slot_idx
                qs_min = max(0, k - 4 * g)
                sbuf_ = sc2[sl % 2]
                bp = p % 2
                while ((p, g) not in qt_tok or (p, k // 4) not in kt_tok
                       or (p, k // 4) not in v_tok):
                    if ln1_done["n"] < NT:
                        ensure_tiles(ln1_done["n"] + 1)
                    pump(1, avail=(ln1_done["n"] - 1)
                         if ln1_done["n"] < NT else 10 ** 9)
                w = [("s_qev", max(qt_tok[(p, g)][0], kt_tok[(p, k // 4)][0])),
                     ("s_xnT", max(qt_tok[(p, g)][1], kt_tok[(p, k // 4)][1]))]
                if sl >= 2:
                    w.append(("s_exp", exp_tok[sl - 2]))
                pl.op("tensor", lambda e, kk=k, gg=g, sc_=sbuf_, b_=bp:
                      e.matmul(sc_[0:P, 0:512],
                               KT2[b_][0:64, kk * P:(kk + 1) * P],
                               QT2[b_][0:64, gg * 512:(gg + 1) * 512],
                               start=True, stop=True), w)
                pl.op("tensor", lambda e, kk=k, gg=g, sc_=sbuf_, b_=bp:
                      e.matmul(sc_[0:P, 512:1024],
                               KT2[b_][64:128, kk * P:(kk + 1) * P],
                               QT2[b_][64:128, gg * 512:(gg + 1) * 512],
                               start=True, stop=True), (), [("s_sc", 1)])
                sc_tok[sl] = pl.val("s_sc")
                wexp = [("s_sc", sc_tok[sl])]
                if sl >= 4:
                    wexp.append(("s_av", av_tok[sl - 4]))
                nq = 512 - qs_min * P
                ptb = PT[sl % 4]
                pl.op("scalar", lambda e, sc_=sbuf_, q0=qs_min * P,
                      nn=nq, pt=ptb: e.activation(
                          bass.AP(pt, q0, [[1024, P], [512, 2], [1, nn]]),
                          bass.AP(sc_, q0, [[1024, P], [512, 2], [1, nn]]),
                          AF.Exp, scale=0.125),
                      wexp, [("s_exp", 1)])
                exp_tok[sl] = pl.val("s_exp")
                diag = (k >= 4 * g)
                msk_tok = 0
                if diag:
                    for hip in range(2):
                        pl.op("vector", lambda e, q0=qs_min * P, h=hip,
                              pt=ptb: e.tensor_tensor(
                                  pt[0:P, h * 512 + q0:h * 512 + q0 + P],
                                  pt[0:P, h * 512 + q0:h * 512 + q0 + P],
                                  tri[:], AluOpType.mult),
                              [("s_exp", exp_tok[sl]), ("scst", t_tri)],
                              [("s_msk", 1)])
                    msk_tok = pl.val("s_msk")
                if len(pend) >= 2:
                    e0 = pend.pop(0)
                    emit_av(e0[0], e0[1])
                pend.append((sl, (k, g, qs_min, ptb, diag, msk_tok)))
                slot_idx += 1
                if ln1_done["n"] < NT:
                    ln1_trickle()
                if k < 4 * g - 4:
                    pump(1, avail=(ln1_done["n"] - 1)
                         if ln1_done["n"] < NT else 10 ** 9,
                         allow_heavy=(k < 4 * g - 7))
            while pend:
                e0 = pend.pop(0)
                emit_av(e0[0], e0[1])
            rb = (g % 2) * 8
            for hip in range(2):
                t_avh = (avh0_tok if hip == 0 else av_tok)[slot_idx - 1]
                pl.op("vector", lambda e, rb_=rb + 4 * hip, h=hip:
                      e.reciprocal(
                          bass.AP(rinv4, rb_, [[16, P], [1, 4]]),
                          bass.AP(av1h[h], 64, [[512, P], [65, 4]])),
                      [("s_av", t_avh)], [("s_misc", 1)])
                t_rcp = pl.val("s_misc")
                pl.op("vector", lambda e, jj=4 * g, pp=p,
                      rb_=rb + 4 * hip, h=hip: e.tensor_tensor(
                          bass.AP(cat, jj * 512 + pp * P + h * 64,
                                  [[NT * 512, P], [512, 4], [1, 64]]),
                          bass.AP(av1h[h], 0, [[512, P], [65, 4], [1, 64]]),
                          bass.AP(rinv4, rb_, [[16, P], [1, 4], [0, 64]]),
                          AluOpType.mult), [("s_misc", t_rcp)],
                      [("s_avev", 1)])
                prev_div[hip] = pl.val("s_avev")
            avev_tok[(p, g)] = pl.val("s_avev")
            if p > 0 and g % 2 == 1 and g < 7:
                enqueue_wo(p, g // 2)
            if (p, g) == (0, 6):
                enqueue_weight_prep()
                state["rstd_tok"] = pl.val("s_rstd")
                for r_ in range(3):
                    enqueue_wo(0, r_)
            if g == 7:
                last_sc_by_pair[p] = sc_tok[slot_idx - 1]
                last_av_by_pair[p] = av_tok[slot_idx - 1]
                enqueue_wo(p, 3)
                if p == 0:
                    enqueue_qkv(2)
                elif p == 2:
                    pass
        flush()

        t_wall = pl.val("s_woev")
        pl.op("vector", lambda e: e.tensor_reduce(
            amaxp[:, 0:1], amaxp[:], AX, AluOpType.max), [("s_woev", t_wall)],
            [("s_stats", 1)])
        t_am = pl.val("s_stats")
        pl.op("gpsimd", lambda e: e.memset(cc8[:], 0.0), (), [("s_gp", 1)])
        t_m8 = pl.val("s_gp")
        drain("gpsimd")
        pl.op("gpsimd", lambda e: e.tensor_reduce(
            cc8[0:1, 0:1], amaxp[:, 0:1], AXC, AluOpType.abs_max),
            [("s_stats", t_am)], [("s_gp", 1)])
        t_cr = pl.val("s_gp")
        drain("gpsimd")
        t_ci = dma("gpsimd", cc_in.ap(), cc8[:], [("s_gp", t_cr)], "sgc")
        pl.op("gpsimd", lambda e: e.collective_compute(
            "AllReduce", AluOpType.max,
            replica_groups=[list(range(n_cores))],
            ins=[cc_in.ap().opt()], outs=[cc_out.ap().opt()]),
            [("sgc", t_ci)], [("scc", 1)])
        t_cc = pl.val("scc")
        t_co = dma("gpsimd", cc8[:], cc_out.ap(), [("scc", t_cc)], "sgc")

        ff_tok = {}
        ffst_tok = {}
        a1_tok = {}
        tr_ftok = {}
        ev_tok = {}
        rstd_fin = state["rstd_tok"]
        ssp_full = (pl.val("ssp0"), pl.val("ssp1"))
        t_exp_all = pl.val("s_exp")
        scev = [[("s_exp", t_exp_all), ("s_exp", t_exp_all)],
                [("s_exp", t_exp_all), ("s_exp", t_exp_all)]]

        def ffn_a1(t):
            spar = t % 4
            b = xt[spar]
            w = xt_free[spar] + [("ssp%d" % (t % 2), sp_tok[t])]
            xt_free[spar] = []
            t_x2 = dma("sync", b[:], x2spd.ap()[t * P:(t + 1) * P, :], w,
                       "sx%d" % spar)
            wq_ = [("sx%d" % spar, t_x2), ("s_rstd", rstd_fin)]
            if t >= 2:
                wq_.append(("s_tr", tr_ftok[t - 2]))
            pl.op("vector", lambda e, bb=b, tt=t, xx=xn32[t % 2]:
                  e.tensor_scalar(xx[:], bb[:, 0:E], stats_m[:, tt:tt + 1],
                                  stats_r[:, tt:tt + 1],
                                  AluOpType.subtract, AluOpType.mult),
                  wq_, [("s_stats", 1)])
            a1_tok[t] = pl.val("s_stats")
            pl.op("vector", lambda e, bb=b: e.tensor_tensor(
                bb[:], bb[:], biasr[:], AluOpType.add), [("scst", t_bias)])

        def ffn_tr(t):
            xx = xn32[t % 2]
            for cb in range(8):
                wv = [("s_stats", a1_tok[t]), ("scst", t_iden)]
                if cb == 0:
                    if t == 0:
                        wv.append(("s_woev", t_wall))
                    elif t == 1:
                        wv.append(pwev[0])
                        wv.append(pwev[1])
                    elif t == 2:
                        wv.append(("s_exp", t_exp_all))
                    else:
                        wv.append(("s_wc", ev_tok[t - 2]))
                if t == 2:
                    dst = sc2[0][0:P, cb * P:(cb + 1) * P]
                elif t % 2 == 0:
                    dst = av1h[cb // 4][0:P, (cb % 4) * P:(cb % 4) * P + P]
                else:
                    dst = pw[cb // 4][0:P, (cb % 4) * P:(cb % 4) * P + P]
                pl.op("tensor", lambda e, d=dst, c=cb, x=xx: e.transpose(
                    d, x[:, c * P:(c + 1) * P], iden[:]),
                    wv, [("s_tr", 1)])
            tr_ftok[t] = pl.val("s_tr")

        def ffn_evac(t):
            qq = q8t[t % 3]
            if t == 2:
                src0, src1 = sc2[0][0:P, 0:512], sc2[0][0:P, 512:1024]
            elif t % 2 == 0:
                src0, src1 = av1h[0][:], av1h[1][:]
            else:
                src0, src1 = pw[0][:], pw[1][:]
            wv = [("s_tr", tr_ftok[t])]
            if t >= 3:
                wv.append(("s_ff", ff_tok[t - 3]))
            pl.op("scalar", lambda e, q=qq, s=src0: e.activation(
                q[:, 0:512], s, AF.Copy, scale=sc128[:, 0:1]),
                wv, [("s_wc", 1)])
            pl.op("scalar", lambda e, q=qq, s=src1: e.activation(
                q[:, 512:1024], s, AF.Copy, scale=sc128[:, 0:1]),
                (), [("s_wc", 1)])
            ev_tok[t] = pl.val("s_wc")

        def ffn_mm(t):
            qq = q8t[t % 3]
            sc_ = sc2[t % 2]
            for c2 in range(4):
                lhs = bass.AP(qq, 2 * c2 * P, [[E, P], [P, 2], [1, P]])
                for nb in range(2):
                    rhs = bass.AP(wq8, 2 * c2 * E + nb * 512,
                                  [[8 * E, P], [E, 2], [1, 512]])
                    wv3 = []
                    if c2 == 0 and nb == 0:
                        wv3 += [("s_wc", ev_tok[t]),
                                ("s_stats", wstate["t_wqb"])]
                        if t == 0:
                            wv3.append(("s_wc", ev_tok[2]))
                    if c2 == 0:
                        wv3.append(scev[t % 2][nb])
                    pl.op("tensor", lambda e, l=lhs, r=rhs, n=nb, cc=c2,
                          s_=sc_: e.matmul(
                              s_[0:P, n * 512:n * 512 + 512], l, r,
                              start=(cc == 0), stop=(cc == 3),
                              perf_mode=MMPM.DoubleRow),
                          wv3, [("s_ff", 1)] if (c2 == 3 and nb == 1) else ())
            ff_tok[t] = pl.val("s_ff")

        def ffn_epi(t):
            spar = t % 4
            b = xt[spar]
            fo = fout[t % 2]
            sc_ = sc2[t % 2]
            w4 = [("s_ff", ff_tok[t]), ("s_wc", t_scb),
                  ("ssp%d" % (t % 2), ssp_full[t % 2])]
            if t >= 2:
                w4.append(("sout%d" % (t % 2), ffst_tok[t - 2]))
            pl.op("vector", lambda e, bb=b, f=fo, s_=sc_:
                  e.scalar_tensor_tensor(
                      f[:, 0:512], s_[0:P, 0:512], sc128[:, 1:2],
                      bb[:, 0:512], AluOpType.mult, AluOpType.add),
                  w4, [("s_misc", 1)])
            scev[t % 2][0] = ("s_misc", pl.val("s_misc"))
            pl.op("vector", lambda e, bb=b, f=fo, s_=sc_:
                  e.scalar_tensor_tensor(
                      f[:, 512:1024], s_[0:P, 512:1024], sc128[:, 1:2],
                      bb[:, 512:1024], AluOpType.mult, AluOpType.add),
                  (), [("s_ffev", 1)])
            t_stt = pl.val("s_ffev")
            scev[t % 2][1] = ("s_ffev", t_stt)
            xt_free[spar].append(("s_ffev", t_stt))
            ffst_tok[t] = dma("gpsimd" if t % 2 == 0 else "scalar",
                              outd[t * P:(t + 1) * P, :], fo[:],
                              [("s_misc", scev[t % 2][0][1]),
                               ("s_ffev", t_stt)], "sout%d" % (t % 2))

        ffn_a1(0)
        ffn_tr(0)
        ffn_a1(1)
        ffn_tr(1)
        ffn_a1(2)
        ffn_tr(2)

        pl.op("vector", lambda e: e.reciprocal(scal[0:1, 6:7], cc8[0:1, 0:1]),
              [("sgc", t_co)])
        drain("vector")
        pl.op("vector", lambda e: e.tensor_scalar(
            scal[0:1, 2:3], scal[0:1, 6:7], FP8MAX / 2, None, AluOpType.mult))
        pl.op("vector", lambda e: e.tensor_tensor(
            scal[0:1, 3:4], cc8[0:1, 0:1], scal[0:1, 1:2], AluOpType.mult))
        drain("vector")
        pl.op("vector", lambda e: e.tensor_scalar(
            scal[0:1, 3:4], scal[0:1, 3:4], float(4.0 / (FP8MAX * FP8MAX)),
            None, AluOpType.mult), (), [("s_stats", 1)])
        t_sc2v = pl.val("s_stats")
        pl.op("tensor", lambda e: e.matmul(
            sc2[1][0:P, 0:2], ones1r[0:1, 0:P], scal[0:1, 2:4],
            start=True, stop=True),
            [("s_stats", t_sc2v), ("s_gp", t_ones),
             scev[1][0], scev[1][1]], [("s_tr", 1)])
        t_bc = pl.val("s_tr")
        pl.op("scalar", lambda e: e.activation(
            sc128[:, 0:2], sc2[1][0:P, 0:2], AF.Copy),
            [("s_tr", t_bc)], [("s_wc", 1)])
        t_scb = pl.val("s_wc")
        drain("scalar")

        ffn_evac(0)
        ffn_evac(1)
        ffn_evac(2)
        for t in range(16):
            ffn_mm(t)
            if t >= 1:
                ffn_epi(t - 1)
            if t + 3 < 16:
                ffn_a1(t + 3)
                ffn_tr(t + 3)
                ffn_evac(t + 3)
        ffn_epi(15)
        pl.op("gpsimd", lambda e: e.wait_ge(sems["sout0"], pl.val("sout0")))
        pl.op("gpsimd", lambda e: e.wait_ge(sems["sout1"], pl.val("sout1")))

        with nc.Block() as block:
            def mk(eng_name):
                def emitter(eng):
                    seen = {}
                    for emit, waits, incs in pl.ops[eng_name]:
                        for semname, val in waits:
                            if val <= 0 or seen.get(semname, -1) >= val:
                                continue
                            seen[semname] = val
                            eng.wait_ge(sems[semname], val)
                        instr = emit(eng)
                        for semname, nn in incs:
                            instr.then_inc(sems[semname], nn)
                return emitter

            block.sync(mk("sync"))
            block.gpsimd(mk("gpsimd"))
            block.vector(mk("vector"))
            block.scalar(mk("scalar"))
            block.tensor(mk("tensor"))

    return nc


def _prep_core(c, x, Wq, Wk, Wv, W_O, wft, biasr, tri, idenf):
    b, hh = c // 2, c % 2
    h0 = 8 * hh
    A0 = 512 * hh
    xb = np.ascontiguousarray(
        np.concatenate([x[b][:, A0:A0 + 512], x[b][:, 512 - A0:1024 - A0]],
                       axis=1))
    arr = x[b].reshape(1024, 4, 1024)[A0:A0 + 512]
    xres = np.ascontiguousarray(
        arr.reshape(4, 128, 4, 1024).transpose(0, 2, 1, 3).reshape(2048, 1024))
    blocks = []
    for W in (Wq, Wk, Wv):
        for p in range(4):
            blocks.append(W[h0 + 2 * p])
            blocks.append(W[h0 + 2 * p + 1])
    wqkv = np.ascontiguousarray(np.concatenate(blocks, axis=0))
    return {
        "xb": xb, "xresd": xres, "wqkvd": wqkv,
        "wod": np.ascontiguousarray(W_O),
        "wftd": wft, "biasd": biasr, "trid": tri, "idend": idenf,
    }


def kernel(L0_attn_x, L0_attn_Wq, L0_attn_Wk, L0_attn_Wv, L0_attn_W_O,
           L0_ffn_weight, L0_ffn_bias):
    x = np.asarray(L0_attn_x, dtype=np.float32)
    Wq = np.asarray(L0_attn_Wq, dtype=np.float32)
    Wk = np.asarray(L0_attn_Wk, dtype=np.float32)
    Wv = np.asarray(L0_attn_Wv, dtype=np.float32)
    W_O = np.asarray(L0_attn_W_O, dtype=np.float32)
    wft = np.ascontiguousarray(np.asarray(L0_ffn_weight, dtype=np.float32).T)
    biasr = np.ascontiguousarray(
        np.broadcast_to(np.asarray(L0_ffn_bias, dtype=np.float32), (P, E)))
    tri = np.triu(np.ones((P, P))).astype(ml_dtypes.bfloat16)
    idenf = np.eye(P, dtype=np.float32)

    nc = build(n_cores=8)
    in_maps = [_prep_core(c, x, Wq, Wk, Wv, W_O, wft, biasr, tri, idenf)
               for c in range(8)]
    res = run_bass_kernel_spmd(nc, in_maps, core_ids=list(range(8)))

    out = np.empty((4, S, E), dtype=np.float32)
    for c in range(8):
        b, hh = c // 2, c % 2
        A0 = 512 * hh
        co = np.asarray(res.results[c]["outd"])
        co = co.reshape(4, 4, 128, E).transpose(0, 2, 1, 3).reshape(512, 4, E)
        out[b].reshape(1024, 4, E)[A0:A0 + 512] = co
    return out


if __name__ == "__main__":
    nc = build(n_cores=8)
    ni = sum(len(v) for v in nc.m.functions[0].instructions.values()) \
        if hasattr(nc.m.functions[0], "instructions") else -1
    print("build ok")


# version 36
# speedup vs baseline: 1.0602x; 1.0602x over previous
import numpy as np
import ml_dtypes
import contextlib
import concourse.bass as bass
import concourse.mybir as mybir
from concourse.alu_op_type import AluOpType
from concourse.bass_utils import run_bass_kernel_spmd

F32 = mybir.dt.float32
BF16 = mybir.dt.bfloat16
FP8 = mybir.dt.float8e4
AF = mybir.ActivationFunctionType
AX = mybir.AxisListType.X
AXC = mybir.AxisListType.C
MMPM = mybir.MatmulPerfMode

P = 128
S = 4096
E = 1024
D = 64
NT = S // P
NPAIR = 4
NG = 8
EPS = 1e-5
FP8MAX = 448.0

ENGINES = ("sync", "gpsimd", "vector", "scalar", "tensor")


class Plan:
    def __init__(self):
        self.ops = {e: [] for e in ENGINES}
        self.cnt = {}

    def inc(self, name, n=1):
        self.cnt.setdefault(name, 0)
        self.cnt[name] += n
        return self.cnt[name]

    def val(self, name):
        return self.cnt.setdefault(name, 0)

    def op(self, eng, emit, waits=(), incs=()):
        for nm, n in incs:
            self.inc(nm, n)
        self.ops[eng].append((emit, tuple(waits), tuple(incs)))


def build(n_cores=8):
    nc = bass.Bass("TRN2", target_bir_lowering=False, debug=False)

    xb = nc.dram_tensor("xb", [S, E], F32, kind="ExternalInput").ap()
    xresd = nc.dram_tensor("xresd", [2048, E], F32, kind="ExternalInput").ap()
    wqkvd = nc.dram_tensor("wqkvd", [12 * P, D], F32, kind="ExternalInput").ap()
    wod = nc.dram_tensor("wod", [E, E], F32, kind="ExternalInput").ap()
    wftd = nc.dram_tensor("wftd", [E, E], F32, kind="ExternalInput").ap()
    biasd = nc.dram_tensor("biasd", [P, E], F32, kind="ExternalInput").ap()
    trid = nc.dram_tensor("trid", [P, P], BF16, kind="ExternalInput").ap()
    idend = nc.dram_tensor("idend", [P, P], F32, kind="ExternalInput").ap()
    outd = nc.dram_tensor("outd", [2048, E], F32, kind="ExternalOutput").ap()

    x2spd = nc.dram_tensor("x2spd", [2048, E], F32)
    amxd = nc.dram_tensor("amxd", [P], F32)
    wamxd = nc.dram_tensor("wamxd", [P], F32)
    scld = nc.dram_tensor("scld", [8], F32)
    cc_in = nc.dram_tensor("cc_in", [1, 8], F32)
    cc_out = nc.dram_tensor("cc_out", [1, 8], F32)
    cw_in = nc.dram_tensor("cw_in", [1, 8], F32)
    cw_out = nc.dram_tensor("cw_out", [1, 8], F32)

    pl = Plan()
    es = contextlib.ExitStack()
    with es:
        Een = es.enter_context

        def sbt(name, shape, dt):
            return Een(nc.sbuf_tensor(name, shape, dt))

        def pst(name, shape, dt):
            return Een(nc.psum_tensor(name, shape, dt))

        xt = [sbt("xt%d" % i, [P, E], F32) for i in range(4)]
        bn12s = [sbt("bn12%d" % i, [P, 12], F32) for i in range(4)]
        mvs = [sbt("mv%d" % i, [P, 2], F32) for i in range(4)]
        ceps = sbt("ceps", [P, 1], F32)
        rstd1s = [sbt("rstd1%d" % i, [P, 1], F32) for i in range(4)]
        xnf = [sbt("xnf%d" % i, [P, 512], F32) for i in range(2)]
        xnT = [sbt("xnT%d" % p, [P, S], BF16) for p in range(NPAIR)]
        wqkvb = sbt("wqkvb", [P, 768], BF16)
        dd = sbt("dd", [P, E], F32)
        wqkvf = dd[0:P, 0:768]
        xqf = dd
        wob = sbt("wob", [P, 8 * E], BF16)
        wq8 = sbt("wq8", [P, 8 * E], FP8)
        biasr = sbt("biasr", [P, E], F32)
        tri = sbt("tri", [P, P], BF16)
        iden = sbt("iden", [P, P], F32)
        QT2 = [sbt("QT0", [P, S], BF16), sbt("QT1", [P, S], BF16)]
        KT2 = [sbt("KT0", [P, S], BF16), sbt("KT1", [P, S], BF16)]
        VS2 = [[sbt("VA0", [P, 32 * 65], BF16), sbt("VB0", [P, 32 * 65], BF16)],
               [sbt("VA1", [P, 32 * 65], BF16), sbt("VB1", [P, 32 * 65], BF16)]]
        PT = [sbt("PT%d" % i, [P, 1024], BF16) for i in range(4)]
        cat = sbt("cat", [P, NT * 512], BF16)
        rinv4 = sbt("rinv4", [P, 16], F32)
        stats_m = sbt("stats_m", [P, 16], F32)
        stats_r = sbt("stats_r", [P, 16], F32)
        amaxp = sbt("amaxp", [P, 16], F32)
        wamaxp = sbt("wamaxp", [P, 8], F32)
        xres = [sbt("xres0", [P, E], F32), sbt("xres1", [P, E], F32)]
        x2t = [sbt("x2t0", [P, E], F32), sbt("x2t1", [P, E], F32)]
        scal = sbt("scal", [1, 16], F32)
        sc128 = sbt("sc128", [P, 4], F32)
        cc8 = sbt("cc8", [1, 8], F32)
        amx1 = sbt("amx1", [1, P], F32)
        q8t = [sbt("q8t0", [P, E], FP8), sbt("q8t1", [P, E], FP8),
               sbt("q8t2", [P, E], FP8)]
        xn32 = [sbt("xn32_%d" % i, [P, E], F32) for i in range(2)]
        ones1r = sbt("ones1r", [1, P], F32)
        fout = x2t

        nc.const_aps.register(F32, EPS, ceps[:])

        sc2 = [pst("sc2a", [P, 1024], F32), pst("sc2b", [P, 1024], F32)]
        av1h = [pst("av1a", [P, 512], F32), pst("av1b", [P, 512], F32)]
        pw = [pst("pw0", [P, 512], F32), pst("pw1", [P, 512], F32)]

        sems = {}
        for n in ("sx0", "sx1", "sx2", "sx3", "scst", "sxr0", "sxr1", "ssp0", "ssp1",
                  "sgc", "sout0", "sout1", "scc", "s_stats", "s_rstd",
                  "s_ln", "s_tr", "s_xnT", "s_qkv", "s_qev", "s_sc",
                  "s_exp", "s_msk", "s_av", "s_avev", "s_wo", "s_woev",
                  "s_ff", "s_ffev", "s_misc", "s_gp", "s_wc", "s_wob"):
            sems[n] = Een(nc.semaphore(n))
            pl.val(n)

        pwev = [("s_misc", 0), ("s_misc", 0)]
        xt_free = [[], [], [], []]
        xnT_tok = [0] * NT
        apply_tok = [0] * NT
        stats_tok = [0] * NT
        rstd_par_tok = [0, 0, 0, 0]
        appl_par_tok = [0, 0, 0, 0]
        tr_tok2 = [0] * NT

        def dma(eng_name, dst, src, waits=(), sem="scst"):
            pl.op(eng_name, lambda e, d=dst, s=src: e.dma_start(out=d, in_=s),
                  waits, [(sem, 16)])
            return pl.val(sem)

        def drain(eng_name):
            pl.op(eng_name, lambda e: e.drain())

        dma("gpsimd", tri[:], trid, (), "scst")
        dma("gpsimd", iden[:], idend, (), "scst")
        dma("gpsimd", biasr[:], biasd, (), "scst")
        dma("scalar", wqkvf,
            bass.AP(wqkvd.tensor, 0, [[64, P], [8192, 12], [1, 64]]), (), "scst")
        t_tri = t_iden = t_bias = t_wqkv = 64
        pl.op("gpsimd", lambda e: e.memset(ceps[:], EPS), (), [("s_gp", 1)])
        t_ceps = pl.val("s_gp")
        pl.op("gpsimd", lambda e: e.memset(ones1r[:], 1.0), (), [("s_gp", 1)])
        t_ones = pl.val("s_gp")
        t_cw = dma("gpsimd", cw_in.ap()[0:1, 0:1], ceps[0:1, 0:1],
                   [("s_gp", t_ceps)], "sgc")
        pl.op("gpsimd", lambda e: e.collective_compute(
            "AllReduce", AluOpType.max,
            replica_groups=[list(range(n_cores))],
            ins=[cw_in.ap()[0:1, 0:1].opt()],
            outs=[cw_out.ap()[0:1, 0:1].opt()]),
            [("sgc", t_cw)], [("scc", 1)])
        for vpair in VS2:
            for v in vpair:
                pl.op("gpsimd", lambda e, vv=v: e.memset(
                    bass.AP(vv, 64, [[65 * 32, P], [65, 32], [1, 1]]), 1.0))
        pl.op("gpsimd", lambda e: e.tensor_copy(wqkvb[:], wqkvf),
              [("scst", t_wqkv)], [("s_gp", 1)])
        t_wqkvb = pl.val("s_gp")

        state = {"qev": 0, "wo_cnt": 0, "rstd_tok": pl.val("s_rstd")}
        last_sc_by_pair = {}
        last_av_by_pair = {}
        qt_tok = {}
        kt_tok = {}
        v_tok = {}
        avev_partial = {}
        sp_tok = [0] * 16
        woev_tok = [0] * 16
        workq = []

        def pump(n=1, avail=10 ** 9, allow_heavy=True):
            cnt = 0
            while (workq and cnt < n and workq[0][0] <= avail
                   and (allow_heavy or not workq[0][1])):
                workq.pop(0)[2]()
                cnt += 1

        def flush():
            while workq:
                workq.pop(0)[2]()

        def enqueue_qkv(p):
            bp = p % 2
            QT, KT, VSp = QT2[bp], KT2[bp], VS2[bp]
            sc_war = last_sc_by_pair.get(p - 2, 0)
            av_war = last_av_by_pair.get(p - 2, 0)

            def qt_slice(proj, dsttile, sl):
                def thunk():
                    co = (proj * 4 + p) * D
                    tokdict = qt_tok if proj == 0 else kt_tok
                    tk = xnT_tok[min(sl * 4 + 3, NT - 1)]
                    lw = [pwev[0], pwev[1], ("s_qev", tk[0]),
                          ("s_xnT", tk[1]), ("s_gp", t_wqkvb)]
                    pl.op("tensor", lambda e, c=co, s0=sl: e.matmul(
                        pw[0][0:64, 0:512], wqkvb[0:64, c:c + D],
                        xnT[p][0:64, s0 * 512:(s0 + 1) * 512],
                        start=True, stop=True), lw)
                    pl.op("tensor", lambda e, c=co, s0=sl: e.matmul(
                        pw[1][0:64, 0:512], wqkvb[64:128, c:c + D],
                        xnT[p][64:128, s0 * 512:(s0 + 1) * 512],
                        start=True, stop=True), (), [("s_qkv", 1)])
                    tq = pl.val("s_qkv")
                    wv = [("s_qkv", tq)]
                    if sc_war:
                        wv.append(("s_sc", sc_war))
                    pl.op("vector", lambda e, d=dsttile, s0=sl: e.tensor_copy(
                        d[0:64, s0 * 512:(s0 + 1) * 512], pw[0][0:64, 0:512]),
                        wv, [("s_qev", 1)])
                    pwev[0] = ("s_qev", pl.val("s_qev"))
                    if p == 0:
                        pl.op("scalar", lambda e, d=dsttile, s0=sl: e.copy(
                            d[64:128, s0 * 512:(s0 + 1) * 512],
                            pw[1][0:64, 0:512]),
                            wv, [("s_xnT", 1)])
                        pwev[1] = ("s_xnT", pl.val("s_xnT"))
                        tokdict[(p, sl)] = (pl.val("s_qev"), pl.val("s_xnT"))
                    else:
                        pl.op("vector", lambda e, d=dsttile, s0=sl: e.tensor_copy(
                            d[64:128, s0 * 512:(s0 + 1) * 512],
                            pw[1][0:64, 0:512]),
                            (), [("s_qev", 1)])
                        pwev[1] = ("s_qev", pl.val("s_qev"))
                        tokdict[(p, sl)] = (pl.val("s_qev"), 0)
                return thunk

            def v_round(rnd):
                def thunk():
                    co = (2 * 4 + p) * D
                    for j in range(4):
                        k = rnd * 4 + j
                        tk = xnT_tok[min(k, NT - 1)]
                        lw = [("s_qev", tk[0]), ("s_xnT", tk[1])]
                        if j == 0:
                            lw += [pwev[0], pwev[1]]
                        pl.op("tensor", lambda e, kk=k, jj=j, c=co: e.matmul(
                            pw[0][0:P, jj * 64:(jj + 1) * 64],
                            xnT[p][0:64, kk * P:(kk + 1) * P],
                            wqkvb[0:64, c:c + D], start=True, stop=True), lw)
                        pl.op("tensor", lambda e, kk=k, jj=j, c=co: e.matmul(
                            pw[1][0:P, jj * 64:(jj + 1) * 64],
                            xnT[p][64:128, kk * P:(kk + 1) * P],
                            wqkvb[64:128, c:c + D], start=True, stop=True),
                            (), [("s_qkv", 1)] if j == 3 else ())
                    tv = pl.val("s_qkv")
                    for hip in range(2):
                        wv = [("s_qkv", tv)]
                        if av_war and rnd == 0:
                            wv.append(("s_av", av_war))
                        pl.op("vector", lambda e, h=hip, r0=rnd: e.tensor_copy(
                            bass.AP(VSp[h], r0 * 4 * 65,
                                    [[65 * 32, P], [65, 4], [1, 64]]),
                            bass.AP(pw[h], 0, [[512, P], [64, 4], [1, 64]])),
                            wv, [("s_qev", 1)])
                        pwev[hip] = ("s_qev", pl.val("s_qev"))
                    v_tok[(p, rnd)] = pl.val("s_qev")
                return thunk

            for sl in range(8):
                workq.append((4 * sl + 3, False, qt_slice(0, QT, sl)))
                workq.append((4 * sl + 3, False, qt_slice(1, KT, sl)))
                workq.append((4 * sl + 3, False, v_round(sl)))


        enqueue_qkv(0)
        ln1_done = {"n": 0}

        def ln1_statpart(i):
            par = i % 4
            buf = xt[par]
            t_x = dma("sync", buf[:], xb[i * P:(i + 1) * P, :], xt_free[par],
                      "sx%d" % par)
            xt_free[par] = []
            bn12, mv, rstd1 = (bn12s[par], mvs[par], rstd1s[par])
            w = [("sx%d" % par, t_x)]
            if i >= 4:
                w += [("s_ln", appl_par_tok[par]),
                      ("s_rstd", rstd_par_tok[par])]
            pl.op("vector", lambda e, b_=buf, z=bn12: e.bn_stats(
                z[:, 0:6], b_[:, 0:512]), w)
            pl.op("vector", lambda e, b_=buf, z=bn12: e.bn_stats(
                z[:, 6:12], b_[:, 512:1024]), (), [("s_misc", 1)])
            t_bs = pl.val("s_misc")
            pl.op("vector", lambda e, z=bn12, m=mv: e.bn_aggr(m[:], z[:]),
                  [("s_misc", t_bs)], [("s_stats", 1)])
            stats_tok[i] = pl.val("s_stats")
            xt_free[par].append(("s_stats", stats_tok[i]))
            pl.op("scalar", lambda e, m=mv, r_=rstd1: e.activation(
                r_[:], m[:, 1:2], AF.Ln, bias=EPS),
                [("s_stats", stats_tok[i]), ("s_gp", t_ceps)],
                [("s_rstd", 1)])
            t_ln = pl.val("s_rstd")
            pl.op("scalar", lambda e, r_=rstd1: e.activation(
                r_[:], r_[:], AF.Exp, scale=-0.5), [("s_rstd", t_ln)],
                [("s_rstd", 1)])
            rstd_par_tok[par] = pl.val("s_rstd")

        def ln1_applypart(i):
            par = i % 4
            buf = xt[par]
            mv, rstd1 = mvs[par], rstd1s[par]
            ob = xnf[i % 2]
            w2 = [("s_rstd", rstd_par_tok[par])]
            if i >= 2:
                w2.append(("s_tr", tr_tok2[i - 2]))
            pl.op("vector", lambda e, b_=buf, o=ob, m=mv, r_=rstd1:
                  e.tensor_scalar(o[:], b_[:, 0:512], m[:, 0:1],
                                  r_[:], AluOpType.subtract,
                                  AluOpType.mult),
                  w2, [("s_ln", 1)])
            apply_tok[i] = pl.val("s_ln")
            appl_par_tok[par] = apply_tok[i]
            xt_free[par].append(("s_ln", apply_tok[i]))
            for cb in range(4):
                wv = [("s_ln", apply_tok[i]), ("scst", t_iden),
                      pwev[cb % 2]]
                pl.op("tensor", lambda e, o=ob, c=cb: e.transpose(
                    pw[c % 2][0:P, (c // 2) * P:(c // 2) * P + P],
                    o[:, c * P:(c + 1) * P], iden[:]),
                    wv, [("s_tr", 1)])
            tr_tok2[i] = pl.val("s_tr")
            for cb in (0, 2):
                pl.op("scalar", lambda e, c=cb, ii=i: e.copy(
                    xnT[c][0:P, ii * P:(ii + 1) * P],
                    pw[0][0:P, (c // 2) * P:(c // 2) * P + P]),
                    [("s_tr", tr_tok2[i])], [("s_xnT", 1)])
            t_ev_d = pl.val("s_xnT")
            for cb in (1, 3):
                pl.op("scalar", lambda e, c=cb, ii=i: e.copy(
                    xnT[c][0:P, ii * P:(ii + 1) * P],
                    pw[1][0:P, (c // 2) * P:(c // 2) * P + P]),
                    [("s_tr", tr_tok2[i])], [("s_xnT", 1)])
            t_ev_a = pl.val("s_xnT")
            xnT_tok[i] = (0, t_ev_a)
            pwev[0] = ("s_xnT", t_ev_d)
            pwev[1] = ("s_xnT", t_ev_a)
            pump(2, avail=i - 1)
            ln1_done["n"] = i + 1

        ln1_started = {"n": 0}

        def ln1_advance():
            if ln1_started["n"] < NT:
                ln1_statpart(ln1_started["n"])
                ln1_started["n"] += 1
                if ln1_started["n"] < NT:
                    ln1_statpart(ln1_started["n"])
                    ln1_started["n"] += 1
            ln1_applypart(ln1_done["n"])

        def ensure_tiles(n):
            while ln1_done["n"] < min(n, NT):
                nxt = ln1_done["n"]
                while ln1_started["n"] < min(nxt + 2, NT):
                    ln1_statpart(ln1_started["n"])
                    ln1_started["n"] += 1
                ln1_applypart(nxt)

        def ln1_trickle():
            for _ in range(2):
                if ln1_started["n"] < NT:
                    ln1_statpart(ln1_started["n"])
                    ln1_started["n"] += 1
                if (ln1_done["n"] < ln1_started["n"] - 1
                        and ln1_done["n"] < NT):
                    ln1_applypart(ln1_done["n"])

        wstate = {"wob_tok": 0, "t_wqb": 0, "t_sw2": 0, "wamax_done": 0}

        def enqueue_weight_prep():
            ln1_free = [("s_ln", apply_tok[NT - 1]),
                        ("s_stats", stats_tok[NT - 1])]
            wtok = {}

            def wo_dma(j):
                def thunk():
                    par = j % 4
                    wtok[("o", j)] = dma(
                        "sync", xt[par][:], wod[j * P:(j + 1) * P, :],
                        xt_free[par] + (ln1_free if j < 4 else []),
                        "sx%d" % par)
                    xt_free[par] = []
                return thunk

            def wo_copy(j):
                def thunk():
                    par = j % 4
                    pl.op("vector", lambda e, jj=j, b_=xt[par]: e.tensor_copy(
                        wob[:, jj * E:(jj + 1) * E], b_[:]),
                        [("sx%d" % par, wtok[("o", j)])], [("s_wob", 1)])
                    xt_free[par].append(("s_wob", pl.val("s_wob")))
                    if j == 7:
                        wstate["wob_tok"] = pl.val("s_wob")
                return thunk

            def wf_dma(j):
                def thunk():
                    par = j % 4
                    wtok[("f", j)] = dma(
                        "sync", xt[par][:], wftd[j * P:(j + 1) * P, :],
                        xt_free[par], "sx%d" % par)
                    xt_free[par] = []
                return thunk

            def wf_use(j):
                def thunk():
                    par = j % 4
                    pl.op("vector", lambda e, jj=j, b_=xt[par]: e.tensor_reduce(
                        wamaxp[:, jj:jj + 1], b_[:], AX, AluOpType.max,
                        apply_absolute_value=True),
                        [("sx%d" % par, wtok[("f", j)])], [("s_stats", 1)])
                    xt_free[par].append(("s_stats", pl.val("s_stats")))
                return thunk

            def wfin_thunk():
                t_wprev = pl.val("s_stats")
                pl.op("vector", lambda e: e.tensor_reduce(
                    wamaxp[:, 0:1], wamaxp[:], AX, AluOpType.max),
                    [("s_stats", t_wprev)], [("s_stats", 1)])
                t_wam2 = pl.val("s_stats")
                t_wb1 = dma("gpsimd", wamxd.ap(), wamaxp[:, 0:1],
                            [("s_stats", t_wam2)], "sgc")
                t_wb2 = dma("gpsimd", amx1[0:1, :],
                            bass.AP(wamxd, 0, [[0, 1], [1, P]]),
                            [("sgc", t_wb1)], "sgc")
                pl.op("vector", lambda e: e.tensor_reduce(
                    scal[0:1, 1:2], amx1[0:1, :], AX, AluOpType.max),
                    [("sgc", t_wb2)], [("s_misc", 1)])
                t_wmx = pl.val("s_misc")
                pl.op("vector", lambda e: e.reciprocal(scal[0:1, 5:6],
                                                       scal[0:1, 1:2]),
                      [("s_misc", t_wmx)], [("s_misc", 1)])
                t_wrc = pl.val("s_misc")
                pl.op("vector", lambda e: e.tensor_scalar(
                    scal[0:1, 4:5], scal[0:1, 5:6], FP8MAX / 2, None,
                    AluOpType.mult), [("s_misc", t_wrc)], [("s_stats", 1)])
                t_swh = pl.val("s_stats")
                t_sw1 = dma("gpsimd", bass.AP(scld, 4, [[0, 1], [1, 1]]),
                            scal[0:1, 4:5], [("s_stats", t_swh)], "sgc")
                wstate["t_sw2"] = dma("gpsimd", sc128[:, 2:3],
                                      bass.AP(scld, 4, [[0, P], [1, 1]]),
                                      [("sgc", t_sw1)], "sgc")

            def wq_dma(j):
                def thunk():
                    par = j % 4
                    wtok[("q", j)] = dma(
                        "sync", xt[par][:], wftd[j * P:(j + 1) * P, :],
                        xt_free[par], "sx%d" % par)
                    xt_free[par] = []
                return thunk

            def wq_use(j):
                def thunk():
                    par = j % 4
                    wq_ = [("sx%d" % par, wtok[("q", j)]),
                           ("sgc", wstate["t_sw2"])]
                    pl.op("vector", lambda e, jj=j, b_=xt[par]:
                          e.tensor_scalar(wq8[:, jj * E:(jj + 1) * E], b_[:],
                                          sc128[:, 2:3], None,
                                          AluOpType.mult),
                          wq_, [("s_stats", 1)])
                    t_q = pl.val("s_stats")
                    xt_free[par].append(("s_stats", t_q))
                    if j == 7:
                        wstate["t_wqb"] = t_q
                return thunk

            def stagger(dmas, uses):
                for j in range(len(dmas)):
                    workq.append((-1, False, dmas[j]))
                    if j > 0:
                        workq.append((-1, False, uses[j - 1]))
                workq.append((-1, False, uses[-1]))

            stagger([wo_dma(j) for j in range(8)],
                    [wo_copy(j) for j in range(8)])
            stagger([wf_dma(j) for j in range(8)],
                    [wf_use(j) for j in range(8)])
            workq.append((-1, False, wfin_thunk))
            stagger([wq_dma(j) for j in range(8)],
                    [wq_use(j) for j in range(8)])

        slot_idx = 0
        exp_tok = {}
        exph0_tok = {}
        av_tok = {}
        avh0_tok = {}
        prev_div = [0, 0]
        sc_tok = {}
        avev_tok = {}
        def enqueue_wo(p, r):
            t = p * 4 + r
            wcnt = state["wo_cnt"]
            state["wo_cnt"] = wcnt + 1
            catev = avev_tok[(p, 2 * r + 1)]
            cell = {}

            def mm_thunk(jj):
                def thunk():
                    if jj == 0:
                        cell["t_xr"] = dma(
                            "sync", xres[wcnt % 2][:],
                            xresd[t * P:(t + 1) * P, :],
                            [("s_woev", woev_tok[wcnt - 2])] if wcnt >= 2
                            else [], "sxr%d" % (wcnt % 2))
                    for nb in range(2):
                        w2 = [("s_avev", catev), ("s_wob", wstate["wob_tok"])]
                        if jj == 0:
                            w2.append(pwev[nb])
                        pl.op("tensor", lambda e, n=nb, j=jj: e.matmul(
                            pw[n][:],
                            cat[0:P, (8 * r + j) * 512 + p * P:
                                (8 * r + j) * 512 + p * P + P],
                            wob[0:P, j * E + n * 512:j * E + n * 512 + 512],
                            start=(j == 0), stop=(j == 7)),
                            w2, [("s_wo", 1)] if (jj == 7 and nb == 1) else ())
                    if jj == 7:
                        cell["t_wo"] = pl.val("s_wo")
                return thunk

            def evac_thunk():
                bnb = wcnt % 2
                bb = x2t[bnb]
                bn12, mv, rstd1 = (bn12s[bnb], mvs[bnb], rstd1s[bnb])
                w3 = [("s_wo", cell["t_wo"]),
                      ("sxr%d" % (wcnt % 2), cell["t_xr"]),
                      ("s_rstd", state["rstd_tok"]),
                      ("s_gp", t_wqkvb),
                      ("s_stats", state.get("amax_tok", 0))]
                if wcnt >= 2:
                    w3.append(("ssp%d" % (wcnt % 2), sp_tok[wcnt - 2]))
                pl.op("vector", lambda e, b_=bb, xr=xres[wcnt % 2]:
                      e.tensor_tensor(b_[:, 0:512], pw[0][:], xr[:, 0:512],
                                      AluOpType.add), w3, [("s_misc", 1)])
                pwev[0] = ("s_misc", pl.val("s_misc"))
                pl.op("vector", lambda e, b_=bb, xr=xres[wcnt % 2]:
                      e.tensor_tensor(b_[:, 512:1024], pw[1][:],
                                      xr[:, 512:1024], AluOpType.add),
                      (), [("s_misc", 1)])
                pwev[1] = ("s_misc", pl.val("s_misc"))
                t_x2w = pl.val("s_misc")
                pl.op("vector", lambda e, b_=bb, z=bn12: e.bn_stats(
                    z[:, 0:6], b_[:, 0:512]), [("s_misc", t_x2w)])
                pl.op("vector", lambda e, b_=bb, z=bn12: e.bn_stats(
                    z[:, 6:12], b_[:, 512:1024]), (), [("s_misc", 1)])
                t_bs2 = pl.val("s_misc")
                pl.op("vector", lambda e, z=bn12, m=mv: e.bn_aggr(m[:], z[:]),
                      [("s_misc", t_bs2)], [("s_stats", 1)])
                t_st = pl.val("s_stats")
                pl.op("vector", lambda e, tt=t, m=mv: e.tensor_copy(
                    stats_m[:, tt:tt + 1], m[:, 0:1]), [("s_stats", t_st)])
                pl.op("scalar", lambda e, m=mv, r_=rstd1: e.activation(
                    r_[:], m[:, 1:2], AF.Ln, bias=EPS),
                    [("s_stats", t_st), ("s_gp", t_ceps)], [("s_rstd", 1)])
                t_ln2 = pl.val("s_rstd")
                pl.op("scalar", lambda e, tt=t, r_=rstd1: e.activation(
                    stats_r[:, tt:tt + 1], r_[:], AF.Exp, scale=-0.5),
                    [("s_rstd", t_ln2)], [("s_rstd", 1)])
                state["rstd_tok"] = pl.val("s_rstd")
                pl.op("vector", lambda e, b_=bb, m=mv: e.tensor_scalar(
                    dd[:], b_[:], m[:, 0:1], None, AluOpType.subtract),
                    (), [("s_misc", 1)])
                t_dd = pl.val("s_misc")
                pl.op("vector", lambda e, tt=t: e.tensor_reduce(
                    amaxp[:, tt:tt + 1], dd[:], AX, AluOpType.max,
                    apply_absolute_value=True), [("s_misc", t_dd)],
                    [("s_misc", 1)])
                t_amx = pl.val("s_misc")
                pl.op("vector", lambda e, tt=t: e.tensor_scalar(
                    amaxp[:, tt:tt + 1], amaxp[:, tt:tt + 1],
                    stats_r[:, tt:tt + 1], None, AluOpType.mult),
                    [("s_rstd", state["rstd_tok"]), ("s_misc", t_amx)],
                    [("s_woev", 1)])
                woev_tok[wcnt] = pl.val("s_woev")
                state["amax_tok"] = pl.val("s_stats")
                sp_tok[wcnt] = dma("gpsimd",
                                   x2spd.ap()[t * P:(t + 1) * P, :], bb[:],
                                   [("s_woev", woev_tok[wcnt])],
                                   "ssp%d" % (wcnt % 2))

            for jj in range(8):
                workq.append((-1, False, mm_thunk(jj)))
            workq.append((-1, True, evac_thunk))

        state["rstd_tok"] = pl.val("s_rstd")

        enqueue_qkv(1)
        sched = ([(0, g) for g in range(7)] + [(1, 0), (0, 7)]
                 + [(1, g) for g in range(1, 8)]
                 + [(2, g) for g in range(8)] + [(3, g) for g in range(8)])
        seen_pair2 = False
        for p, g in sched:
            if p == 2 and not seen_pair2:
                seen_pair2 = True
                enqueue_qkv(3)
            if p == 0:
                ensure_tiles(4 * g + 4)
            if ln1_done["n"] < NT:
                pump(100, avail=ln1_done["n"] - 1)
            pend = []

            def emit_av(sl0, ctx):
                kk0, gg0, qsm0, pt0, diag0, msk0 = ctx
                for hip in range(2):
                    for qs in range(qsm0, 4):
                        wav = []
                        if qs == qsm0:
                            wav.append(("s_exp", exp_tok[sl0]))
                            wav.append(("s_qev", v_tok[(p, kk0 // 4)]))
                            if diag0:
                                wav.append(("s_msk", msk0))
                            if kk0 == 0:
                                wav.append(("s_avev", prev_div[hip]))
                        last_of_hip = (qs == 3)
                        pl.op("tensor", lambda e, h=hip, q=qs, kk=kk0,
                              gg=gg0, pt=pt0, bp=p % 2: e.matmul(
                                  av1h[h][0:P, q * 65:q * 65 + 65],
                                  pt[0:P, h * 512 + q * P:
                                     h * 512 + (q + 1) * P],
                                  VS2[bp][h][0:P, kk * 65:(kk + 1) * 65],
                                  start=(kk == 0 and q == 0),
                                  stop=(kk == 4 * gg + 3 and q == 3)),
                              wav, [("s_av", 1)] if last_of_hip else ())
                        if last_of_hip and hip == 0:
                            avh0_tok[sl0] = pl.val("s_av")
                av_tok[sl0] = pl.val("s_av")

            for k in range(4 * g + 4):
                sl = slot_idx
                qs_min = max(0, k - 4 * g)
                sbuf_ = sc2[sl % 2]
                bp = p % 2
                while ((p, g) not in qt_tok or (p, k // 4) not in kt_tok
                       or (p, k // 4) not in v_tok):
                    if ln1_done["n"] < NT:
                        ensure_tiles(ln1_done["n"] + 1)
                    pump(1, avail=(ln1_done["n"] - 1)
                         if ln1_done["n"] < NT else 10 ** 9)
                w = [("s_qev", max(qt_tok[(p, g)][0], kt_tok[(p, k // 4)][0])),
                     ("s_xnT", max(qt_tok[(p, g)][1], kt_tok[(p, k // 4)][1]))]
                if sl >= 2:
                    w.append(("s_exp", exp_tok[sl - 2]))
                pl.op("tensor", lambda e, kk=k, gg=g, sc_=sbuf_, b_=bp:
                      e.matmul(sc_[0:P, 0:512],
                               KT2[b_][0:64, kk * P:(kk + 1) * P],
                               QT2[b_][0:64, gg * 512:(gg + 1) * 512],
                               start=True, stop=True), w)
                pl.op("tensor", lambda e, kk=k, gg=g, sc_=sbuf_, b_=bp:
                      e.matmul(sc_[0:P, 512:1024],
                               KT2[b_][64:128, kk * P:(kk + 1) * P],
                               QT2[b_][64:128, gg * 512:(gg + 1) * 512],
                               start=True, stop=True), (), [("s_sc", 1)])
                sc_tok[sl] = pl.val("s_sc")
                wexp = [("s_sc", sc_tok[sl])]
                if sl >= 4:
                    wexp.append(("s_av", av_tok[sl - 4]))
                nq = 512 - qs_min * P
                ptb = PT[sl % 4]
                pl.op("scalar", lambda e, sc_=sbuf_, q0=qs_min * P,
                      nn=nq, pt=ptb: e.activation(
                          bass.AP(pt, q0, [[1024, P], [512, 2], [1, nn]]),
                          bass.AP(sc_, q0, [[1024, P], [512, 2], [1, nn]]),
                          AF.Exp, scale=0.125),
                      wexp, [("s_exp", 1)])
                exp_tok[sl] = pl.val("s_exp")
                diag = (k >= 4 * g)
                msk_tok = 0
                if diag:
                    for hip in range(2):
                        pl.op("vector", lambda e, q0=qs_min * P, h=hip,
                              pt=ptb: e.tensor_tensor(
                                  pt[0:P, h * 512 + q0:h * 512 + q0 + P],
                                  pt[0:P, h * 512 + q0:h * 512 + q0 + P],
                                  tri[:], AluOpType.mult),
                              [("s_exp", exp_tok[sl]), ("scst", t_tri)],
                              [("s_msk", 1)])
                    msk_tok = pl.val("s_msk")
                if len(pend) >= 2:
                    e0 = pend.pop(0)
                    emit_av(e0[0], e0[1])
                pend.append((sl, (k, g, qs_min, ptb, diag, msk_tok)))
                slot_idx += 1
                if ln1_done["n"] < NT:
                    ln1_trickle()
                if k < 4 * g - 4:
                    pump(1, avail=(ln1_done["n"] - 1)
                         if ln1_done["n"] < NT else 10 ** 9,
                         allow_heavy=(k < 4 * g - 7))
            while pend:
                e0 = pend.pop(0)
                emit_av(e0[0], e0[1])
            rb = (g % 2) * 8
            for hip in range(2):
                t_avh = (avh0_tok if hip == 0 else av_tok)[slot_idx - 1]
                pl.op("vector", lambda e, rb_=rb + 4 * hip, h=hip:
                      e.reciprocal(
                          bass.AP(rinv4, rb_, [[16, P], [1, 4]]),
                          bass.AP(av1h[h], 64, [[512, P], [65, 4]])),
                      [("s_av", t_avh)], [("s_misc", 1)])
                t_rcp = pl.val("s_misc")
                pl.op("vector", lambda e, jj=4 * g, pp=p,
                      rb_=rb + 4 * hip, h=hip: e.tensor_tensor(
                          bass.AP(cat, jj * 512 + pp * P + h * 64,
                                  [[NT * 512, P], [512, 4], [1, 64]]),
                          bass.AP(av1h[h], 0, [[512, P], [65, 4], [1, 64]]),
                          bass.AP(rinv4, rb_, [[16, P], [1, 4], [0, 64]]),
                          AluOpType.mult), [("s_misc", t_rcp)],
                      [("s_avev", 1)])
                prev_div[hip] = pl.val("s_avev")
            avev_tok[(p, g)] = pl.val("s_avev")
            if p > 0 and g % 2 == 1 and g < 7:
                enqueue_wo(p, g // 2)
            if (p, g) == (0, 6):
                enqueue_weight_prep()
                state["rstd_tok"] = pl.val("s_rstd")
                for r_ in range(3):
                    enqueue_wo(0, r_)
            if g == 7:
                last_sc_by_pair[p] = sc_tok[slot_idx - 1]
                last_av_by_pair[p] = av_tok[slot_idx - 1]
                enqueue_wo(p, 3)
                if p == 0:
                    enqueue_qkv(2)
                elif p == 2:
                    pass
        flush()

        t_wall = pl.val("s_woev")
        pl.op("vector", lambda e: e.tensor_reduce(
            amaxp[:, 0:1], amaxp[:], AX, AluOpType.max), [("s_woev", t_wall)],
            [("s_stats", 1)])
        t_am = pl.val("s_stats")
        pl.op("gpsimd", lambda e: e.memset(cc8[:], 0.0), (), [("s_gp", 1)])
        t_m8 = pl.val("s_gp")
        drain("gpsimd")
        pl.op("gpsimd", lambda e: e.tensor_reduce(
            cc8[0:1, 0:1], amaxp[:, 0:1], AXC, AluOpType.abs_max),
            [("s_stats", t_am)], [("s_gp", 1)])
        t_cr = pl.val("s_gp")
        drain("gpsimd")
        t_ci = dma("gpsimd", cc_in.ap(), cc8[:], [("s_gp", t_cr)], "sgc")
        pl.op("gpsimd", lambda e: e.collective_compute(
            "AllReduce", AluOpType.max,
            replica_groups=[list(range(n_cores))],
            ins=[cc_in.ap().opt()], outs=[cc_out.ap().opt()]),
            [("sgc", t_ci)], [("scc", 1)])
        t_cc = pl.val("scc")
        t_co = dma("gpsimd", cc8[:], cc_out.ap(), [("scc", t_cc)], "sgc")

        ff_tok = {}
        ffst_tok = {}
        a1_tok = {}
        tr_ftok = {}
        ev_tok = {}
        rstd_fin = state["rstd_tok"]
        ssp_full = (pl.val("ssp0"), pl.val("ssp1"))
        t_exp_all = pl.val("s_exp")
        scev = [[("s_exp", t_exp_all), ("s_exp", t_exp_all)],
                [("s_exp", t_exp_all), ("s_exp", t_exp_all)]]

        XBUF = [xt[0], xt[1], xt[2], xt[3], xres[0], xres[1]]
        XSEM = ["sx0", "sx1", "sx2", "sx3", "sxr0", "sxr1"]
        xfree6 = [xt_free[0], xt_free[1], xt_free[2], xt_free[3],
                  [("s_woev", t_wall)], [("s_woev", t_wall)]]

        def ffn_a1(t):
            p6 = t % 6
            b = XBUF[p6]
            w = xfree6[p6] + [("ssp%d" % (t % 2), sp_tok[t])]
            xfree6[p6] = []
            t_x2 = dma("sync", b[:], x2spd.ap()[t * P:(t + 1) * P, :], w,
                       XSEM[p6])
            wq_ = [(XSEM[p6], t_x2), ("s_rstd", rstd_fin)]
            if t >= 2:
                wq_.append(("s_tr", tr_ftok[t - 2]))
            pl.op("vector", lambda e, bb=b, tt=t, xx=xn32[t % 2]:
                  e.tensor_scalar(xx[:], bb[:, 0:E], stats_m[:, tt:tt + 1],
                                  stats_r[:, tt:tt + 1],
                                  AluOpType.subtract, AluOpType.mult),
                  wq_, [("s_stats", 1)])
            a1_tok[t] = pl.val("s_stats")
            pl.op("vector", lambda e, bb=b: e.tensor_tensor(
                bb[:], bb[:], biasr[:], AluOpType.add), [("scst", t_bias)])

        def ffn_tr(t):
            xx = xn32[t % 2]
            for cb in range(8):
                wv = [("s_stats", a1_tok[t]), ("scst", t_iden)]
                if cb == 0:
                    if t == 0:
                        wv.append(("s_woev", t_wall))
                    elif t == 1:
                        wv.append(pwev[0])
                        wv.append(pwev[1])
                    elif t == 2:
                        wv.append(("s_exp", t_exp_all))
                    else:
                        wv.append(("s_wc", ev_tok[t - 2]))
                if t == 2:
                    dst = sc2[0][0:P, cb * P:(cb + 1) * P]
                elif t % 2 == 0:
                    dst = av1h[cb // 4][0:P, (cb % 4) * P:(cb % 4) * P + P]
                else:
                    dst = pw[cb // 4][0:P, (cb % 4) * P:(cb % 4) * P + P]
                pl.op("tensor", lambda e, d=dst, c=cb, x=xx: e.transpose(
                    d, x[:, c * P:(c + 1) * P], iden[:]),
                    wv, [("s_tr", 1)])
            tr_ftok[t] = pl.val("s_tr")

        def ffn_evac(t):
            qq = q8t[t % 3]
            if t == 2:
                src0, src1 = sc2[0][0:P, 0:512], sc2[0][0:P, 512:1024]
            elif t % 2 == 0:
                src0, src1 = av1h[0][:], av1h[1][:]
            else:
                src0, src1 = pw[0][:], pw[1][:]
            wv = [("s_tr", tr_ftok[t])]
            if t >= 3:
                wv.append(("s_ff", ff_tok[t - 3]))
            pl.op("scalar", lambda e, q=qq, s=src0: e.activation(
                q[:, 0:512], s, AF.Copy, scale=sc128[:, 0:1]),
                wv, [("s_wc", 1)])
            pl.op("scalar", lambda e, q=qq, s=src1: e.activation(
                q[:, 512:1024], s, AF.Copy, scale=sc128[:, 0:1]),
                (), [("s_wc", 1)])
            ev_tok[t] = pl.val("s_wc")

        def ffn_mm(t):
            qq = q8t[t % 3]
            sc_ = sc2[t % 2]
            for c2 in range(4):
                lhs = bass.AP(qq, 2 * c2 * P, [[E, P], [P, 2], [1, P]])
                for nb in range(2):
                    rhs = bass.AP(wq8, 2 * c2 * E + nb * 512,
                                  [[8 * E, P], [E, 2], [1, 512]])
                    wv3 = []
                    if c2 == 0 and nb == 0:
                        wv3 += [("s_wc", ev_tok[t]),
                                ("s_stats", wstate["t_wqb"])]
                        if t == 0:
                            wv3.append(("s_wc", ev_tok[2]))
                    if c2 == 0:
                        wv3.append(scev[t % 2][nb])
                    pl.op("tensor", lambda e, l=lhs, r=rhs, n=nb, cc=c2,
                          s_=sc_: e.matmul(
                              s_[0:P, n * 512:n * 512 + 512], l, r,
                              start=(cc == 0), stop=(cc == 3),
                              perf_mode=MMPM.DoubleRow),
                          wv3, [("s_ff", 1)] if (c2 == 3 and nb == 1) else ())
            ff_tok[t] = pl.val("s_ff")

        def ffn_epi(t):
            p6 = t % 6
            b = XBUF[p6]
            fo = fout[t % 2]
            sc_ = sc2[t % 2]
            w4 = [("s_ff", ff_tok[t]), ("s_wc", t_scb),
                  ("ssp%d" % (t % 2), ssp_full[t % 2])]
            if t >= 2:
                w4.append(("sout%d" % (t % 2), ffst_tok[t - 2]))
            pl.op("vector", lambda e, bb=b, f=fo, s_=sc_:
                  e.scalar_tensor_tensor(
                      f[:, 0:512], s_[0:P, 0:512], sc128[:, 1:2],
                      bb[:, 0:512], AluOpType.mult, AluOpType.add),
                  w4, [("s_misc", 1)])
            scev[t % 2][0] = ("s_misc", pl.val("s_misc"))
            pl.op("vector", lambda e, bb=b, f=fo, s_=sc_:
                  e.scalar_tensor_tensor(
                      f[:, 512:1024], s_[0:P, 512:1024], sc128[:, 1:2],
                      bb[:, 512:1024], AluOpType.mult, AluOpType.add),
                  (), [("s_ffev", 1)])
            t_stt = pl.val("s_ffev")
            scev[t % 2][1] = ("s_ffev", t_stt)
            xfree6[p6].append(("s_ffev", t_stt))
            ffst_tok[t] = dma("gpsimd",
                              outd[t * P:(t + 1) * P, :], fo[:],
                              [("s_misc", scev[t % 2][0][1]),
                               ("s_ffev", t_stt)], "sout%d" % (t % 2))

        ffn_a1(0)
        ffn_tr(0)
        ffn_a1(1)
        ffn_tr(1)
        ffn_a1(2)
        ffn_tr(2)

        pl.op("vector", lambda e: e.reciprocal(scal[0:1, 6:7], cc8[0:1, 0:1]),
              [("sgc", t_co)])
        drain("vector")
        pl.op("vector", lambda e: e.tensor_scalar(
            scal[0:1, 2:3], scal[0:1, 6:7], FP8MAX / 2, None, AluOpType.mult))
        pl.op("vector", lambda e: e.tensor_tensor(
            scal[0:1, 3:4], cc8[0:1, 0:1], scal[0:1, 1:2], AluOpType.mult))
        drain("vector")
        pl.op("vector", lambda e: e.tensor_scalar(
            scal[0:1, 3:4], scal[0:1, 3:4], float(4.0 / (FP8MAX * FP8MAX)),
            None, AluOpType.mult), (), [("s_stats", 1)])
        t_sc2v = pl.val("s_stats")
        pl.op("tensor", lambda e: e.matmul(
            sc2[1][0:P, 0:2], ones1r[0:1, 0:P], scal[0:1, 2:4],
            start=True, stop=True),
            [("s_stats", t_sc2v), ("s_gp", t_ones),
             scev[1][0], scev[1][1]], [("s_tr", 1)])
        t_bc = pl.val("s_tr")
        pl.op("scalar", lambda e: e.activation(
            sc128[:, 0:2], sc2[1][0:P, 0:2], AF.Copy),
            [("s_tr", t_bc)], [("s_wc", 1)])
        t_scb = pl.val("s_wc")
        drain("scalar")

        ffn_evac(0)
        ffn_evac(1)
        ffn_evac(2)
        for t in range(16):
            ffn_mm(t)
            if t >= 1:
                ffn_epi(t - 1)
            if t + 3 < 16:
                ffn_a1(t + 3)
                ffn_tr(t + 3)
                ffn_evac(t + 3)
        ffn_epi(15)
        pl.op("gpsimd", lambda e: e.wait_ge(sems["sout0"], pl.val("sout0")))
        pl.op("gpsimd", lambda e: e.wait_ge(sems["sout1"], pl.val("sout1")))

        with nc.Block() as block:
            def mk(eng_name):
                def emitter(eng):
                    seen = {}
                    for emit, waits, incs in pl.ops[eng_name]:
                        for semname, val in waits:
                            if val <= 0 or seen.get(semname, -1) >= val:
                                continue
                            seen[semname] = val
                            eng.wait_ge(sems[semname], val)
                        instr = emit(eng)
                        for semname, nn in incs:
                            instr.then_inc(sems[semname], nn)
                return emitter

            block.sync(mk("sync"))
            block.gpsimd(mk("gpsimd"))
            block.vector(mk("vector"))
            block.scalar(mk("scalar"))
            block.tensor(mk("tensor"))

    return nc


def _prep_core(c, x, Wq, Wk, Wv, W_O, wft, biasr, tri, idenf):
    b, hh = c // 2, c % 2
    h0 = 8 * hh
    A0 = 512 * hh
    xb = np.ascontiguousarray(
        np.concatenate([x[b][:, A0:A0 + 512], x[b][:, 512 - A0:1024 - A0]],
                       axis=1))
    arr = x[b].reshape(1024, 4, 1024)[A0:A0 + 512]
    xres = np.ascontiguousarray(
        arr.reshape(4, 128, 4, 1024).transpose(0, 2, 1, 3).reshape(2048, 1024))
    blocks = []
    for W in (Wq, Wk, Wv):
        for p in range(4):
            blocks.append(W[h0 + 2 * p])
            blocks.append(W[h0 + 2 * p + 1])
    wqkv = np.ascontiguousarray(np.concatenate(blocks, axis=0))
    return {
        "xb": xb, "xresd": xres, "wqkvd": wqkv,
        "wod": np.ascontiguousarray(W_O),
        "wftd": wft, "biasd": biasr, "trid": tri, "idend": idenf,
    }


def kernel(L0_attn_x, L0_attn_Wq, L0_attn_Wk, L0_attn_Wv, L0_attn_W_O,
           L0_ffn_weight, L0_ffn_bias):
    x = np.asarray(L0_attn_x, dtype=np.float32)
    Wq = np.asarray(L0_attn_Wq, dtype=np.float32)
    Wk = np.asarray(L0_attn_Wk, dtype=np.float32)
    Wv = np.asarray(L0_attn_Wv, dtype=np.float32)
    W_O = np.asarray(L0_attn_W_O, dtype=np.float32)
    wft = np.ascontiguousarray(np.asarray(L0_ffn_weight, dtype=np.float32).T)
    biasr = np.ascontiguousarray(
        np.broadcast_to(np.asarray(L0_ffn_bias, dtype=np.float32), (P, E)))
    tri = np.triu(np.ones((P, P))).astype(ml_dtypes.bfloat16)
    idenf = np.eye(P, dtype=np.float32)

    nc = build(n_cores=8)
    in_maps = [_prep_core(c, x, Wq, Wk, Wv, W_O, wft, biasr, tri, idenf)
               for c in range(8)]
    res = run_bass_kernel_spmd(nc, in_maps, core_ids=list(range(8)))

    out = np.empty((4, S, E), dtype=np.float32)
    for c in range(8):
        b, hh = c // 2, c % 2
        A0 = 512 * hh
        co = np.asarray(res.results[c]["outd"])
        co = co.reshape(4, 4, 128, E).transpose(0, 2, 1, 3).reshape(512, 4, E)
        out[b].reshape(1024, 4, E)[A0:A0 + 512] = co
    return out


if __name__ == "__main__":
    nc = build(n_cores=8)
    ni = sum(len(v) for v in nc.m.functions[0].instructions.values()) \
        if hasattr(nc.m.functions[0], "instructions") else -1
    print("build ok")
